# version 03
# baseline (speedup 1.0000x reference)
import contextlib
import numpy as np
import ml_dtypes
import concourse.bass as bass
import concourse.mybir as mybir
from concourse.bass_utils import run_bass_kernel_spmd

F32 = mybir.dt.float32
BF16 = mybir.dt.bfloat16
AF = mybir.ActivationFunctionType
ALU = mybir.AluOpType
AX = mybir.AxisListType

H = 2048
I_MOE = 1024
I_SH = 4096
NE = 8
I_ATT = 4096
NH = 64
PD = 64
NST = 64
K_CONV = 4
CONV_DIM = I_ATT + 2 * NST
PROJ = I_ATT + CONV_DIM + NH
EPS = 1e-5
NCC = CONV_DIM // 128


class Eng:
    def __init__(self, p, name, obj):
        self.p = p
        self.name = name
        self.o = obj
        self.waited = {}

    def wait(self, tok):
        if tok is None:
            return
        sem, val = tok
        if self.waited.get(sem, 0) >= val:
            return
        self.o.wait_ge(self.p.sems[sem], val)
        self.waited[sem] = val

    def waits(self, toks):
        for t in toks:
            self.wait(t)


class P:

    def __init__(self, nc):
        self.nc = nc
        self.es = contextlib.ExitStack()
        self.sems = {}
        self.counts = {}
        self.pe = Eng(self, "pe", nc.tensor)
        self.ve = Eng(self, "ve", nc.vector)
        self.se = Eng(self, "se", nc.scalar)
        self.po = Eng(self, "po", nc.gpsimd)
        self.sy = Eng(self, "sy", nc.sync)

    def sem(self, name):
        if name not in self.sems:
            self.sems[name] = self.es.enter_context(self.nc.semaphore(name))
            self.counts[name] = 0
        return self.sems[name]

    def tok(self, ins, name, amt=1):
        self.sem(name)
        self.counts[name] += amt
        ins.then_inc(self.sems[name], amt)
        return (name, self.counts[name])

    def sb(self, name, shape, dtype):
        return self.es.enter_context(self.nc.sbuf_tensor(name, shape, dtype))

    def ps(self, name, shape, dtype):
        return self.es.enter_context(self.nc.psum_tensor(name, shape, dtype))


class Slot:

    def __init__(self):
        self.ready = None
        self.frees = []

    def produce(self, eng, toks=()):
        eng.waits(self.frees)
        eng.waits(toks)
        self.frees = []

    def consume(self, eng):
        eng.wait(self.ready)


def ceil_div(a, b):
    return (a + b - 1) // b



def build_ffn(p, B, xr_in, xr_out, T, tag):
    nc = p.nc
    TH = T // 2
    NTC = TH // 128
    units = [("moe", e) for e in range(NE)] + [("sh", q) for q in range(4)]
    sp = B["spacer"]

    def spacer_ve():
        nc.vector.tensor_scalar_add(sp[:, 0:4], sp[:, 0:4], 0.0)

    for half in range(2):
        t0 = half * TH
        st2 = B["st2"]
        stm_ = B["stm"]
        ln_dmat_toks = []
        router_mm_toks = []

        def ln_reduces(tc):
            xs = B["xr_slot"][tc % 2]
            xs.produce(p.sy)
            ld = p.tok(nc.sync.dma_start(
                out=B["xr_tile"][tc % 2][:, :],
                in_=xr_in[t0 + tc * 128: t0 + (tc + 1) * 128, :]), "q_ln", 16)
            xs.ready = ld
            xs.consume(p.ve)
            nc.vector.reduce_sum(st2[:, (tc % 2) * 4:(tc % 2) * 4 + 1],
                                 B["xr_tile"][tc % 2][:, :], axis=AX.X)
            xs.consume(p.se)
            du = B["dump_slot"][0]
            du.produce(p.se)
            t_sq = p.tok(nc.scalar.activation(B["dump"][0][:, :],
                                              B["xr_tile"][tc % 2][:, :], AF.Square), "e_se")
            du.ready = t_sq
            xs.frees.append(t_sq)
            du.consume(p.ve)
            t_r2 = p.tok(nc.vector.reduce_sum(st2[:, (tc % 2) * 4 + 1:(tc % 2) * 4 + 2],
                                              B["dump"][0][:, :], axis=AX.X), "e_ve")
            du.frees.append(t_r2)
            return xs

        def ln_chain(tc):
            sl = (tc % 2) * 4
            spacer_ve()
            t_m = p.tok(nc.vector.tensor_scalar_mul(
                stm_[:, tc, 0:1], st2[:, sl:sl + 1], 1.0 / H), "e_ve")
            p.se.wait(t_m)
            t_q = p.tok(nc.scalar.activation(stm_[:, tc, 1:2], stm_[:, tc, 0:1],
                                             AF.Square), "e_se")
            p.ve.wait(t_q)
            t_v = p.tok(nc.vector.scalar_tensor_tensor(
                stm_[:, tc, 2:3], st2[:, sl + 1:sl + 2], 1.0 / H,
                stm_[:, tc, 1:2], ALU.mult, ALU.subtract), "e_ve")
            p.se.wait(t_v)
            nc.scalar.activation(st2[:, 3:4], stm_[:, tc, 2:3], AF.Ln, bias=B["cst"][:, 0:1])
            nc.scalar.copy(st2[:, 7:8], stm_[:, tc, 0:1])
            t_r = p.tok(nc.scalar.activation(stm_[:, tc, 3:4], st2[:, 3:4],
                                             AF.Exp, scale=-0.5), "e_se")
            return t_r

        def ln_apply(tc, xs, t_r):
            p.ve.wait(t_r)
            lt = B["lnt_slot"][tc % 2]
            lt.produce(p.ve)
            tv = p.tok(nc.vector.tensor_scalar(
                B["lnt_tile"][tc % 2][:, :], B["xr_tile"][tc % 2][:, :],
                stm_[:, tc, 0:1], stm_[:, tc, 3:4], ALU.subtract, ALU.mult), "e_ve")
            lt.ready = tv
            xs.frees.append(tv)
            lt.consume(p.sy)
            p.sy.waits(B["xln_war"])
            for dq in range(4):
                td = p.tok(nc.sync.dma_start(
                    out=B["xln_fm"][:, 4 * dq:4 * dq + 4, tc * 128:(tc + 1) * 128],
                    in_=B["lnt_tile"][tc % 2][:, dq * 512:(dq + 1) * 512],
                    transpose=True), "q_dt", 16)
            lt.frees.append(td)
            ln_dmat_toks.append(td)
            if "router" not in PARTS:
                return
            p.pe.wait(td)
            p.pe.waits(B["psum_free"][7])
            for dc in range(16):
                ins = nc.tensor.matmul(
                    B["psum"][7][:, tc * 8:tc * 8 + 8],
                    B["xln_fm"][:, dc, tc * 128:(tc + 1) * 128],
                    B["wg_sb"][:, dc, :], start=(dc == 0), stop=(dc == 15))
            router_mm_toks.append(p.tok(ins, "e_pe"))

        prev = None
        for tc in range(NTC):
            xs = ln_reduces(tc)
            if prev is not None:
                ln_apply(prev[0], prev[1], prev[2])
            t_r = ln_chain(tc)
            prev = (tc, xs, t_r)
        ln_apply(prev[0], prev[1], prev[2])
        B["xln_war"] = []
        B["psum_free"][7] = []
        _dbg_dump(p, B, f"{tag}h{half}_ln")

        if "router" not in PARTS:
            continue
        lg = B["lg"]
        lgb = B["lgb"]
        c_half = B["c_half"]
        p.ve.waits(router_mm_toks)
        nc.vector.tensor_copy(lgb[:, 0:NTC, :], B["psum"][7][:, 0:NTC * 8].rearrange(
            "p (t e) -> p t e", e=8))
        spacer_ve()
        spacer_ve()
        nc.vector.reduce_max(lg[:, 0:16], lgb[:, :, :], axis=AX.X)
        spacer_ve()
        m1b = bass.AP(lg, 0, [[512, 128], [1, 16], [0, 8]])
        m2b = bass.AP(lg, 16, [[512, 128], [1, 16], [0, 8]])
        zzb = bass.AP(lg, 448, [[512, 128], [1, 16], [0, 8]])
        
        e1v = lg[:, 64:192].rearrange("p (t e) -> p t e", e=8)
        lgmv = lg[:, 192:320].rearrange("p (t e) -> p t e", e=8)
        e2v = lg[:, 320:448].rearrange("p (t e) -> p t e", e=8)
        nc.vector.tensor_tensor(e1v, lgb[:, :, :], m1b, ALU.is_equal)
        nc.vector.scalar_tensor_tensor(lgmv, e1v, -1e30, lgb[:, :, :],
                                       ALU.mult, ALU.add)
        nc.vector.reduce_max(lg[:, 16:32], lgmv, axis=AX.X)
        spacer_ve()
        nc.vector.tensor_tensor(e2v, lgmv, m2b, ALU.is_equal)
        t_d = p.tok(nc.vector.tensor_sub(lg[:, 32:48], lg[:, 16:32],
                                         lg[:, 0:16]), "e_ve")
        p.se.wait(t_d)
        t_z = p.tok(nc.scalar.activation(lg[:, 448:464], lg[:, 32:48], AF.Exp), "e_se")
        p.ve.wait(t_z)
        nc.vector.tensor_scalar_add(lg[:, 464:480], lg[:, 448:464], 1.0)
        nc.vector.tensor_tensor(e2v, e2v, zzb, ALU.mult)
        nc.vector.reciprocal(lg[:, 480:496], lg[:, 464:480])
        nc.vector.tensor_add(e1v, e1v, e2v)
        t_c = p.tok(nc.vector.scalar_tensor_tensor(
            c_half[:, 0:NTC, :],
            e1v[:, 0:NTC, :] if NTC < 16 else e1v, 0.5,
            bass.AP(lg, 480, [[512, 128], [1, NTC], [0, 8]]),
            ALU.mult, ALU.mult), "e_ve")
        B["psum_free"][7] = [t_c]
        _dbg_dump(p, B, f"{tag}h{half}_router")

        for tc in range(NTC):
            macc = p.tok(nc.vector.memset(B["acc"][:, tc, :], 0.0), "e_ve")

        for ui, (kind, idx) in enumerate(units):
            if kind == "moe":
                gu_src = lambda fc: B["w_gu"][idx].rearrange("(dc p) f -> p dc f", p=128)[:, :, fc * 512:(fc + 1) * 512]
                g_cols = [0, 1]
                u_cols = [2, 3]
                dn_src = lambda ft: B["w_dn"][idx].rearrange("(ic p) d -> p ic d", p=128)[:, :, ft * 512:(ft + 1) * 512]
            else:
                q = idx
                goff = q * 2
                uoff = 8 + q * 2
                gu_src = lambda fc: B["w_sh_gu"].rearrange("(dc p) f -> p dc f", p=128)[:, :, fc * 512:(fc + 1) * 512]
                g_cols = [goff, goff + 1]
                u_cols = [uoff, uoff + 1]
                dn_src = lambda ft: B["w_sh_dn"].rearrange("(ic p) d -> p ic d", p=128)[:, q * 8:q * 8 + 8, ft * 512:(ft + 1) * 512]

            act_dmat_toks = []
            for fp in range(2):
                for gi, fc in enumerate([g_cols[fp], u_cols[fp]]):
                    ws = B["wgu_slot"][gi]
                    ws.produce(p.po)
                    wtok = p.tok(nc.gpsimd.dma_start(
                        out=B["wgu"][gi][:, :, :], in_=gu_src(fc)), "q_wgu", 16)
                    ws.ready = wtok
                for tc in range(NTC):
                    bsel = (tc % 2) * 2
                    pg, pu = B["psum"][bsel], B["psum"][bsel + 1]
                    wg_s = B["wgu_slot"][0]
                    wu_s = B["wgu_slot"][1]
                    p.pe.waits(B["psum_free"][bsel]); B["psum_free"][bsel] = []
                    p.pe.waits(B["psum_free"][bsel + 1]); B["psum_free"][bsel + 1] = []
                    wg_s.consume(p.pe)
                    wu_s.consume(p.pe)
                    for dc in range(16):
                        nc.tensor.matmul(pg[:, :], B["xln_fm"][:, dc, tc * 128:(tc + 1) * 128],
                                         B["wgu"][0][:, dc, :], start=(dc == 0), stop=(dc == 15))
                    for dc in range(16):
                        ins = nc.tensor.matmul(pu[:, :], B["xln_fm"][:, dc, tc * 128:(tc + 1) * 128],
                                               B["wgu"][1][:, dc, :], start=(dc == 0), stop=(dc == 15))
                    tmm = p.tok(ins, "e_pe")
                    wg_s.frees.append(tmm)
                    wu_s.frees.append(tmm)
                    if tc == NTC - 1:
                        B["xln_war"].append(tmm)
                    sg = B["sg_slot"][0]
                    sg.produce(p.se, [tmm])
                    tse = p.tok(nc.scalar.activation(B["sg"][0][:, :], pg[:, :], AF.Silu), "e_se")
                    sg.ready = tse
                    at = B["acttm_slot"][tc % 2]
                    at.produce(p.ve, [tmm, tse])
                    if kind == "moe":
                        scal = B["c_half"][:, tc, idx:idx + 1]
                    else:
                        scal = 0.5
                    tve = p.tok(nc.vector.scalar_tensor_tensor(
                        B["acttm"][tc % 2][:, :], pu[:, :], scal, B["sg"][0][:, :],
                        ALU.mult, ALU.mult), "e_ve")
                    at.ready = tve
                    sg.frees.append(tve)
                    B["psum_free"][bsel] = [tse, tve]
                    B["psum_free"][bsel + 1] = [tve]
                    at.consume(p.sy)
                    p.sy.waits(B["actfm_war"])
                    tdt = p.tok(nc.sync.dma_start(
                        out=B["act_fm"][:, 4 * fp:4 * fp + 4, tc * 128:(tc + 1) * 128],
                        in_=B["acttm"][tc % 2][:, :], transpose=True), "q_dt", 16)
                    at.frees.append(tdt)
                    act_dmat_toks.append(tdt)
            B["actfm_war"] = []

            p.pe.waits(act_dmat_toks)
            for ft in range(4):
                ws = B["wdn_slot"][ft % 2]
                ws.produce(p.po)
                wtok = p.tok(nc.gpsimd.dma_start(
                    out=B["wdn"][ft % 2][:, :, :], in_=dn_src(ft)), "q_wdn", 16)
                ws.ready = wtok
                ws.consume(p.pe)
                for tc in range(NTC):
                    bsel = 4 + (tc % 2) * 2 + (ft % 2)
                    pd_ = B["psum"][bsel]
                    p.pe.waits(B["psum_free"][bsel]); B["psum_free"][bsel] = []
                    for ic in range(8):
                        ins = nc.tensor.matmul(pd_[:, :], B["act_fm"][:, ic, tc * 128:(tc + 1) * 128],
                                               B["wdn"][ft % 2][:, ic, :], start=(ic == 0), stop=(ic == 7))
                    tmm = p.tok(ins, "e_pe")
                    ws.frees.append(tmm)
                    if ft == 3 and tc == NTC - 1:
                        B["actfm_war"].append(tmm)
                    p.ve.wait(tmm)
                    tve = p.tok(nc.vector.scalar_tensor_tensor(
                        B["acc"][:, tc, ft * 512:(ft + 1) * 512], pd_[:, :], 1.0,
                        B["acc"][:, tc, ft * 512:(ft + 1) * 512], ALU.mult, ALU.add), "e_ve")
                    B["psum_free"][bsel] = [tve]
            B["actfm_war"].append(tmm)

        _dbg_dump(p, B, f"{tag}h{half}_units")
        if "final" not in PARTS:
            continue
        for tc in range(NTC):
            xs = B["xr_slot"][tc % 2]
            xs.produce(p.sy)
            ld = p.tok(nc.sync.dma_start(
                out=B["xr_tile"][tc % 2][:, :],
                in_=xr_in[t0 + tc * 128: t0 + (tc + 1) * 128, :]), "q_ln", 16)
            xs.ready = ld
            xs.consume(p.ve)
            tve = p.tok(nc.vector.tensor_add(
                B["xr_tile"][tc % 2][:, :], B["xr_tile"][tc % 2][:, :],
                B["acc"][:, tc, :]), "e_ve")
            p.sy.wait(tve)
            tst = p.tok(nc.sync.dma_start(
                out=xr_out[t0 + tc * 128: t0 + (tc + 1) * 128, :],
                in_=B["xr_tile"][tc % 2][:, :]), "q_st", 16)
            xs.frees.append(tst)
            B["out_toks"].append(tst)



DBG = []
MSTOP = ""
PARTS = ("ln", "router", "units", "final")


def _dbg_dump(p, B, point):
    nc = p.nc
    for i, (pt, key) in enumerate(DBG):
        if pt != point:
            continue
        buf = B[key] if isinstance(key, str) else key
        shp = list(buf.ap().shape)
        name = f"dbg{i}_{pt}_{key if isinstance(key, str) else 'x'}"
        dt = nc.dram_tensor(name, shp, buf.dtype if hasattr(buf, 'dtype') else F32,
                            kind="ExternalOutput")
        for s, v in list(p.counts.items()):
            p.po.wait((s, v))
        tok = p.tok(nc.gpsimd.dma_start(out=dt.ap(), in_=buf.ap()), "q_dbg", 16)
        p.po.wait(tok)
        B["out_toks"].append(tok)


def build_mamba(p, B, xr_in, xr_out, T, dbgm=None):
    nc = p.nc
    NTC = T // 128
    NQ = 4
    sp = B["spacer"]

    def spacer_ve():
        nc.vector.tensor_scalar_add(sp[:, 0:4], sp[:, 0:4], 0.0)

    def phase_barrier():
        for eng in (p.pe, p.ve, p.se, p.po, p.sy):
            for s, v in list(p.counts.items()):
                eng.wait((s, v))

    esM = contextlib.ExitStack()
    p_sb_save = p.sb
    p.sb = lambda n, s, d: esM.enter_context(nc.sbuf_tensor(n, s, d))
    dt_tm = p.sb("mb_dt", [128, NTC, 64], F32)
    dt_bf = p.sb("mb_dtb", [128, NTC, 64], BF16)
    acum_tm = p.sb("mb_acum_tm", [128, NTC, 64], F32)
    sd_tm = p.sb("mb_sd", [128, NTC, 64], F32)
    ds_bf = p.sb("mb_dsb", [128, NTC, 64], BF16)
    dtds_bf = p.sb("mb_dtds", [128, NTC, 64], BF16)
    dA_fm = p.sb("mb_dA", [64, T], F32)
    acum_fm = p.sb("mb_acumf", [64, T], F32)
    bnds = p.sb("mb_bnds", [64, 32], F32)
    bndsT = p.sb("mb_bndsT", [32, 64], F32)
    nc.vector.memset(bnds[:, :], 0.0)
    dec_rep = p.sb("mb_dec", [64, 1024], F32)
    chain_rep = p.sb("mb_chain", [64, 1024], BF16)
    ones64 = p.sb("mb_ones", [64, 128], F32)
    bc_fm = p.sb("mb_bc", [128, T], BF16)
    c_fm = p.sb("mb_cfm", [64, T], BF16)
    b_tm = p.sb("mb_btm", [128, NTC, 64], BF16)
    a_rep = p.sb("mb_arep", [128, 64], F32)
    dtb_rep = p.sb("mb_dtbr", [128, 64], F32)
    d_repb = p.sb("mb_drep", [128, 64], BF16)
    idf = p.sb("mb_idf", [128, 128], F32)
    mask01 = p.sb("mb_mask", [128, 128], F32)
    oddm = p.sb("mb_oddm", [128, 8], F32)
    sliver = p.sb("mb_sliv", [128, NCC, 8], BF16)
    cw_sb = p.sb("mb_cw", [128, 4, NCC + 31], F32)
    cb_sb = p.sb("mb_cb", [128, NCC + 31], F32)

    nc.vector.memset(ones64[:, :], 1.0)
    nc.vector.memset(sliver[:, :, :], 0.0)
    t0 = p.tok(nc.gpsimd.dma_start(out=idf[:, :], in_=B["d_consts"][0:128, 0:128]), "q_mb", 16)
    p.tok(nc.gpsimd.dma_start(out=mask01[:, :], in_=B["d_consts"][0:128, 128:256]), "q_mb", 16)
    p.tok(nc.gpsimd.dma_start(out=oddm[:, :], in_=B["d_consts"][0:128, 256:264]), "q_mb", 16)
    p.tok(nc.gpsimd.dma_start(out=a_rep[:, :],
                              in_=bass.AP(B["d_alog"], 0, [[0, 128], [1, 64]])), "q_mb", 16)
    p.tok(nc.gpsimd.dma_start(out=dtb_rep[:, :],
                              in_=bass.AP(B["d_dtb"], 0, [[0, 128], [1, 64]])), "q_mb", 16)
    tmb = p.tok(nc.gpsimd.dma_start(out=d_repb[:, :],
                                    in_=bass.AP(B["d_dvec"], 0, [[0, 128], [1, 64]])), "q_mb", 16)
    p.se.wait(tmb)
    t_a = p.tok(nc.scalar.activation(a_rep[:, :], a_rep[:, :], AF.Exp), "e_se")
    p.ve.wait(t_a)
    t_a2 = p.tok(nc.vector.tensor_scalar_mul(a_rep[:, :], a_rep[:, :], -1.0), "e_ve")

    with p.nc.sbuf_tensor("mb_cwtmp", [NCC, 128], F32) as cwtmp:
        for k in range(4):
            tk = p.tok(nc.gpsimd.dma_start(out=cwtmp[:, :], in_=B["d_convw"][k, :, :]), "q_mb", 16)
            p.pe.wait(tk)
            p.pe.wait(t0)
            p.pe.waits(B["psum_free"][6]); B["psum_free"][6] = []
            tpe = p.tok(nc.tensor.transpose(B["psum"][6][:, 0:NCC], cwtmp[:, :], idf[0:NCC, 0:NCC]), "e_pe")
            p.ve.wait(tpe)
            tve = p.tok(nc.vector.tensor_copy(cw_sb[:, k, 0:NCC], B["psum"][6][:, 0:NCC]), "e_ve")
            B["psum_free"][6] = [tve]
            p.po.wait(tve)
        tk = p.tok(nc.gpsimd.dma_start(out=cwtmp[:, :], in_=B["d_convb"][:, :]), "q_mb", 16)
        p.pe.wait(tk)
        p.pe.waits(B["psum_free"][6]); B["psum_free"][6] = []
        tpe = p.tok(nc.tensor.transpose(B["psum"][6][:, 0:NCC], cwtmp[:, :], idf[0:NCC, 0:NCC]), "e_pe")
        p.ve.wait(tpe)
        tve = p.tok(nc.vector.tensor_copy(cb_sb[:, 0:NCC], B["psum"][6][:, 0:NCC]), "e_ve")
        B["psum_free"][6] = [tve]
        p.po.wait(tve)

    if MSTOP == "pre":
        esM.close()
        p.sb = p_sb_save
        return
    with contextlib.ExitStack() as esA:
        sbA = lambda n, s, d: esA.enter_context(nc.sbuf_tensor(n, s, d))
        g_rep = sbA("ma_grep", [128, H], F32)
        xln2 = sbA("ma_xln2", [128, 16, H], BF16)
        xrt = [sbA(f"ma_xr{i}", [128, H], F32) for i in range(2)]
        xrs = [Slot() for _ in range(2)]
        t1b = sbA("ma_t1", [128, H], F32)
        lnt = [sbA(f"ma_lnt{i}", [128, H], BF16) for i in range(2)]
        lns = [Slot() for _ in range(2)]
        dmp = sbA("ma_dump", [128, H], BF16)
        dmps = Slot()
        stm_ = sbA("ma_stm", [128, 16, 4], F32)
        st2 = B["st2"]
        wri = [sbA(f"ma_w{i}", [128, 16, 256], BF16) for i in range(2)]
        wrs = [Slot() for _ in range(2)]
        evt = [sbA(f"ma_ev{i}", [128, 512], BF16) for i in range(4)]
        evs = [Slot() for _ in range(4)]
        xbcT = sbA("ma_xbcT", [128, 2, NTC, 128], BF16)
        xbcTs = Slot()
        dtraw = sbA("ma_dtraw", [128, NTC, 64], F32)
        dfl = sbA("ma_dfl", [128, NTC, 64], F32)

        tg = p.tok(nc.gpsimd.dma_start(out=g_rep[:, :],
                                       in_=bass.AP(B["d_lnw"], 0, [[0, 128], [1, H]])), "q_mb", 16)

        ln_dmat = []

        def ln2_reduce(tc):
            xs = xrs[tc % 2]
            xs.produce(p.sy)
            ld = p.tok(nc.sync.dma_start(out=xrt[tc % 2][:, :],
                                         in_=xr_in[tc * 128:(tc + 1) * 128, :]), "q_ln", 16)
            xs.ready = ld
            xs.consume(p.ve)
            nc.vector.reduce_sum(st2[:, (tc % 2) * 4:(tc % 2) * 4 + 1], xrt[tc % 2][:, :], axis=AX.X)
            xs.consume(p.se)
            dmps.produce(p.se)
            t_sq = p.tok(nc.scalar.activation(dmp[:, :], xrt[tc % 2][:, :], AF.Square), "e_se")
            dmps.ready = t_sq
            xs.frees.append(t_sq)
            dmps.consume(p.ve)
            t_r2 = p.tok(nc.vector.reduce_sum(st2[:, (tc % 2) * 4 + 1:(tc % 2) * 4 + 2],
                                              dmp[:, :], axis=AX.X), "e_ve")
            dmps.frees.append(t_r2)
            return xs

        def ln2_chain(tc):
            sl = (tc % 2) * 4
            spacer_ve()
            t_m = p.tok(nc.vector.tensor_scalar_mul(stm_[:, tc, 0:1], st2[:, sl:sl + 1], 1.0 / H), "e_ve")
            p.se.wait(t_m)
            t_q = p.tok(nc.scalar.activation(stm_[:, tc, 1:2], stm_[:, tc, 0:1], AF.Square), "e_se")
            p.ve.wait(t_q)
            t_v = p.tok(nc.vector.scalar_tensor_tensor(
                stm_[:, tc, 2:3], st2[:, sl + 1:sl + 2], 1.0 / H,
                stm_[:, tc, 1:2], ALU.mult, ALU.subtract), "e_ve")
            p.se.wait(t_v)
            nc.scalar.activation(st2[:, 3:4], stm_[:, tc, 2:3], AF.Ln, bias=B["cst"][:, 0:1])
            nc.scalar.copy(st2[:, 7:8], stm_[:, tc, 0:1])
            t_r = p.tok(nc.scalar.activation(stm_[:, tc, 3:4], st2[:, 3:4], AF.Exp, scale=-0.5), "e_se")
            return t_r

        def ln2_apply(tc, xs, t_r):
            p.ve.wait(t_r)
            p.ve.wait(tg)
            nc.vector.tensor_scalar(t1b[:, :], xrt[tc % 2][:, :],
                                    stm_[:, tc, 0:1], stm_[:, tc, 3:4], ALU.subtract, ALU.mult)
            lt = lns[tc % 2]
            lt.produce(p.ve)
            tv = p.tok(nc.vector.tensor_mul(lnt[tc % 2][:, :], t1b[:, :], g_rep[:, :]), "e_ve")
            lt.ready = tv
            t_x2 = p.tok(nc.vector.tensor_add(xrt[tc % 2][:, :], xrt[tc % 2][:, :], t1b[:, :]), "e_ve")
            xs.frees.append(t_x2)
            p.sy.wait(t_x2)
            tst = p.tok(nc.sync.dma_start(out=xr_out[tc * 128:(tc + 1) * 128, :],
                                          in_=xrt[tc % 2][:, :]), "q_st", 16)
            xs.frees.append(tst)
            lt.consume(p.sy)
            for dq in range(4):
                td = p.tok(nc.sync.dma_start(
                    out=xln2[:, 4 * dq:4 * dq + 4, tc * 128:(tc + 1) * 128],
                    in_=lnt[tc % 2][:, dq * 512:(dq + 1) * 512], transpose=True), "q_dt", 16)
            lt.frees.append(td)
            ln_dmat.append(td)

        prev = None
        for tc in range(NTC):
            xs = ln2_reduce(tc)
            if prev is not None:
                ln2_apply(prev[0], prev[1], prev[2])
            t_r = ln2_chain(tc)
            prev = (tc, xs, t_r)
        ln2_apply(prev[0], prev[1], prev[2])

        if MSTOP == "A1":
            esA.close(); esM.close(); p.sb = p_sb_save
            return
        p.pe.waits(ln_dmat)
        FT = [(f * 256, 256) for f in range(32)] + [(8192, 192)]
        ev_i = 0
        for fi, (f0, fw) in enumerate(FT):
            ws = wrs[fi % 2]
            ws.produce(p.po)
            wtok = p.tok(nc.gpsimd.dma_start(
                out=wri[fi % 2][:, :, 0:fw],
                in_=B["d_winT"].rearrange("(dc p) f -> p dc f", p=128)[:, :, f0:f0 + fw]), "q_wgu", 16)
            ws.ready = wtok
            ws.consume(p.pe)
            is_xbc = 16 <= fi < 32
            if is_xbc:
                xbcTs.produce(p.sy)
            for tc in range(NTC):
                bsel = tc % 4
                pd_ = B["psum"][bsel]
                p.pe.waits(B["psum_free"][bsel]); B["psum_free"][bsel] = []
                if fw == 192:
                    p.pe.waits(B["psum_free"][7]); B["psum_free"][7] = []
                    for dc in range(16):
                        nc.tensor.matmul(pd_[:, 0:128], xln2[:, dc, tc * 128:(tc + 1) * 128],
                                         wri[fi % 2][:, dc, 0:128], start=(dc == 0), stop=(dc == 15))
                    for dc in range(16):
                        ins = nc.tensor.matmul(B["psum"][7][:, 0:64],
                                               xln2[:, dc, tc * 128:(tc + 1) * 128],
                                               wri[fi % 2][:, dc, 128:192], start=(dc == 0), stop=(dc == 15))
                else:
                    for dc in range(16):
                        ins = nc.tensor.matmul(pd_[:, 0:fw], xln2[:, dc, tc * 128:(tc + 1) * 128],
                                               wri[fi % 2][:, dc, 0:fw], start=(dc == 0), stop=(dc == 15))
                tmm = p.tok(ins, "e_pe")
                if tc == 15:
                    ws.frees.append(tmm)
                if fi < 16:
                    ev = evs[ev_i % 4]
                    ev.produce(p.ve, [tmm])
                    tve = p.tok(nc.vector.tensor_copy(evt[ev_i % 4][:, 0:256], pd_[:, 0:256]), "e_ve")
                    ev.ready = tve
                    B["psum_free"][bsel] = [tve]
                    ev.consume(p.po)
                    tstg = p.tok(nc.gpsimd.dma_start(
                        out=B["gate_sp"][tc * 128:(tc + 1) * 128, f0:f0 + 256],
                        in_=evt[ev_i % 4][:, 0:256]), "q_st2", 16)
                    ev.frees.append(tstg)
                    ev_i += 1
                elif fi < 32:
                    ev = evs[ev_i % 4]
                    ev.produce(p.se, [tmm])
                    tse = p.tok(nc.scalar.copy(evt[ev_i % 4][:, 0:256], pd_[:, 0:256]), "e_se")
                    ev.ready = tse
                    B["psum_free"][bsel] = [tse]
                    ev.consume(p.sy)
                    tdt = p.tok(nc.sync.dma_start(out=xbcT[:, :, tc, :], in_=evt[ev_i % 4][:, 0:256],
                                                  transpose=True), "q_dt", 16)
                    ev.frees.append(tdt)
                    xbcTs.ready = tdt
                    ev_i += 1
                else:
                    ev = evs[ev_i % 4]
                    ev.produce(p.se, [tmm])
                    tse = p.tok(nc.scalar.copy(evt[ev_i % 4][:, 0:128], pd_[:, 0:128]), "e_se")
                    ev.ready = tse
                    p.ve.wait(tmm)
                    if fw == 192 and MSTOP != "A2b":
                        tvd = p.tok(nc.vector.tensor_copy(dtraw[:, tc, :],
                                                          B["psum"][7][:, 0:64]), "e_ve")
                        B["psum_free"][7] = [tvd]
                    else:
                        tvd = p.tok(nc.vector.tensor_copy(sp[:, 0:4], sp[:, 0:4]), "e_ve")
                    B["psum_free"][bsel] = [tse, tvd]
                    ev.consume(p.sy)
                    tdt = p.tok(nc.sync.dma_start(out=xbcT[:, 0, tc, :], in_=evt[ev_i % 4][:, 0:128],
                                                  transpose=True), "q_dt", 16)
                    ev.frees.append(tdt)
                    xbcTs.ready = tdt
                    ev_i += 1
            if is_xbc or fi == 32:
                ncc_here = 2 if is_xbc else 1
                cc0 = (fi - 16) * 2 if is_xbc else 32
                p.ve.wait(xbcTs.ready)
                for cc in range(ncc_here):
                    nc.vector.tensor_copy(sliver[:, cc0 + cc, 0:8], xbcT[:, cc, NTC - 1, 120:128])
                tslv = p.tok(nc.vector.tensor_scalar_add(sp[:, 0:4], sp[:, 0:4], 0.0), "e_ve")
                p.po.wait(xbcTs.ready)
                for cc in range(ncc_here):
                    tstg = p.tok(nc.gpsimd.dma_start(
                        out=B["xbc_sp"][cc0 + cc], in_=xbcT[:, cc, :, :]), "q_st2", 16)
                xbcTs.frees = [tstg, tslv]

        if MSTOP == "A2":
            esA.close(); esM.close(); p.sb = p_sb_save
            return
        p.ve.wait(tvd)
        p.ve.wait(t_a2)
        dtb_b = bass.AP(dtb_rep, 0, [[64, 128], [0, NTC], [1, 64]])
        a_b = bass.AP(a_rep, 0, [[64, 128], [0, NTC], [1, 64]])
        t_s = p.tok(nc.vector.tensor_tensor(dtraw[:, :, :], dtraw[:, :, :], dtb_b, ALU.add), "e_ve")
        t_mx = p.tok(nc.vector.tensor_scalar_max(dt_tm[:, :, :], dtraw[:, :, :], 0.0), "e_ve")
        p.se.wait(t_mx)
        nc.scalar.activation(dfl[:, :, :], dtraw[:, :, :], AF.Abs)
        nc.scalar.activation(dtraw[:, :, :], dfl[:, :, :], AF.Exp, scale=-1.0)
        t_l = p.tok(nc.scalar.activation(dfl[:, :, :], dtraw[:, :, :], AF.Ln,
                                         bias=B["cst"][:, 1:2]), "e_se")
        p.ve.wait(t_l)
        nc.vector.tensor_add(dt_tm[:, :, :], dt_tm[:, :, :], dfl[:, :, :])
        spacer_ve()
        nc.vector.tensor_copy(dt_bf[:, :, :], dt_tm[:, :, :])
        t_da = p.tok(nc.vector.tensor_tensor(dfl[:, :, :], dt_tm[:, :, :], a_b, ALU.mult), "e_ve")
        for tc in range(NTC):
            p.pe.wait(t_da)
            p.pe.waits(B["psum_free"][4 + tc % 2]); B["psum_free"][4 + tc % 2] = []
            tpe = p.tok(nc.tensor.transpose(B["psum"][4 + tc % 2][0:64, 0:128],
                                            dfl[:, tc, :], idf[:, :]), "e_pe")
            p.ve.wait(tpe)
            t_da_pe_last = tpe
            tve = p.tok(nc.vector.tensor_copy(dA_fm[:, tc * 128:(tc + 1) * 128],
                                              B["psum"][4 + tc % 2][0:64, 0:128]), "e_ve")
            B["psum_free"][4 + tc % 2] = [tve]
        if MSTOP == "A3":
            esA.close(); esM.close(); p.sb = p_sb_save
            return
        for tc in range(NTC):
            nc.vector.tensor_tensor_scan(acum_fm[:, tc * 128:(tc + 1) * 128],
                                         ones64[:, :], dA_fm[:, tc * 128:(tc + 1) * 128],
                                         0.0, ALU.mult, ALU.add)
        t_sc = p.tok(nc.vector.tensor_scalar_add(sp[:, 0:4], sp[:, 0:4], 0.0), "e_ve")
        spacer_ve()
        t_bn = p.tok(nc.vector.tensor_copy(
            bnds[:, 0:NTC], bass.AP(acum_fm, 127, [[T, 64], [128, NTC]])), "e_ve")
        p.sy.wait(t_sc)
        t_asp = p.tok(nc.sync.dma_start(
            out=B["acum_sp"][0:NTC].rearrange("c (hh l) -> hh c l", hh=64),
            in_=acum_fm[:, :].rearrange("hh (c l) -> hh c l", l=128)), "q_st", 16)
        for tc in range(NTC):
            p.pe.wait(t_sc)
            p.pe.waits(B["psum_free"][4 + tc % 2]); B["psum_free"][4 + tc % 2] = []
            tpe = p.tok(nc.tensor.transpose(B["psum"][4 + tc % 2][0:128, 0:64],
                                            acum_fm[:, tc * 128:(tc + 1) * 128], idf[0:64, 0:64]), "e_pe")
            p.ve.wait(tpe)
            tve = p.tok(nc.vector.tensor_copy(acum_tm[:, tc, :], B["psum"][4 + tc % 2][:, 0:64]), "e_ve")
            B["psum_free"][4 + tc % 2] = [tve]
        spacer_ve()
        t_sd0 = p.tok(nc.vector.tensor_scalar_add(sp[:, 0:4], sp[:, 0:4], 0.0), "e_ve")
        p.se.wait(t_sd0)
        t_sd = p.tok(nc.scalar.activation(sd_tm[:, :, :], acum_tm[:, :, :], AF.Exp), "e_se")
        spacer_ve()
        nc.vector.memset(bnds[:, 16:17], 0.0)
        if NTC > 1:
            nc.vector.tensor_tensor_scan(bnds[:, 17:16 + NTC], ones64[:, 0:NTC - 1],
                                         bnds[:, 0:NTC - 1], 0.0, ALU.mult, ALU.add)
        spacer_ve()
        t_bn2 = p.tok(nc.vector.tensor_copy(sp[:, 0:4], sp[:, 0:4]), "e_ve")
        p.pe.wait(t_bn2)
        p.pe.waits(B["psum_free"][6]); B["psum_free"][6] = []
        t_bt2 = p.tok(nc.tensor.transpose(B["psum"][6][0:32, 0:64], bnds[:, 0:32],
                                          idf[0:64, 0:64]), "e_pe")
        p.ve.wait(t_bt2)
        bcT = dfl
        t_bcv = p.tok(nc.vector.tensor_copy(bndsT[:, :], B["psum"][6][0:32, 0:64]), "e_ve")
        B["psum_free"][6] = [t_bcv]
        p.sy.wait(t_bcv)
        t_bsp = p.tok(nc.sync.dma_start(
            out=B["bnd_sp"].rearrange("(c hh) -> c hh", hh=64)[0:NTC, :],
            in_=bndsT[0:NTC, :]), "q_st", 16)
        p.po.wait(t_bsp)
        p.po.wait(t_da_pe_last)
        bndr = dfl
        t_br = p.tok(nc.gpsimd.dma_start(
            out=bndr[:, :, :],
            in_=bass.AP(B["bnd_sp"], 0, [[0, 128], [1, 16], [16, 64]]) if False else
                bass.AP(B["bnd_sp"], 0, [[0, 128], [1, 1024]]).rearrange(
                    "p (c hh) -> p c hh", hh=64)[:, 0:NTC, :]),
            "q_mb", 16)
        p.ve.wait(t_br)
        p.ve.wait(t_sd)
        t_dsd = p.tok(nc.vector.tensor_sub(bndr[:, :, :], bndr[:, :, :], acum_tm[:, :, :]), "e_ve")
        p.se.wait(t_dsd)
        t_dse = p.tok(nc.scalar.activation(ds_bf[:, :, :], bndr[:, :, :], AF.Exp), "e_se")
        p.ve.wait(t_dse)
        nc.vector.tensor_mul(dtds_bf[:, :, :], dt_bf[:, :, :], ds_bf[:, :, :])
        t_dr = p.tok(nc.gpsimd.dma_start(
            out=dec_rep[:, :],
            in_=bass.AP(B["bnd_sp"], 0, [[0, 64], [1, 1024]])), "q_mb", 16)
        p.se.wait(t_dr)
        t_de = p.tok(nc.scalar.activation(dec_rep[:, :], dec_rep[:, :], AF.Exp), "e_se")
        t_csp = p.tok(nc.sync.dma_start(
            out=B["chain_sp"].rearrange("(c hh) -> c hh", hh=64)[0:NTC, :],
            in_=bndsT[16:16 + NTC, :]), "q_st", 16)
        p.po.wait(t_csp)
        t_cr = p.tok(nc.gpsimd.dma_start(
            out=chain_rep[:, :],
            in_=bass.AP(B["chain_sp"], 0, [[0, 64], [1, 1024]])), "q_mb", 16)
        p.se.wait(t_cr)
        t_ce = p.tok(nc.scalar.activation(chain_rep[:, :], chain_rep[:, :], AF.Exp), "e_se")
        p.ve.wait(t_ce)
        p.ve.wait(t_de)

    phase_barrier()

    if MSTOP == "A":
        esM.close()
        p.sb = p_sb_save
        return
    with contextlib.ExitStack() as esB:
        sbB = lambda n, s, d: esB.enter_context(nc.sbuf_tensor(n, s, d))
        ext = [sbB(f"mc_ext{i}", [128, T + 8], BF16) for i in range(2)]
        exts = [Slot() for _ in range(2)]
        cacc = sbB("mc_acc", [128, T], F32)
        xst = [sbB(f"mc_xs{i}", [128, T], BF16) for i in range(2)]
        xsts = [Slot() for _ in range(2)]
        xtmT = sbB("mc_xtmT", [128, NTC, 8, 128], BF16)
        xtmTs = Slot()
        sliv_in = sbB("mc_slin", [128, NCC * 8], BF16)

        tsl = p.tok(nc.gpsimd.dma_start(out=B["cc1_in"].ap(), in_=sliver[:, :, :]), "q_mb", 16)
        p.po.wait(tsl)
        tcc = p.tok(nc.gpsimd.collective_compute(
            "AllGather", ALU.bypass, replica_groups=[[0, 1], [2, 3], [4, 5], [6, 7]],
            ins=[B["cc1_in"].ap()], outs=[B["cc1_out"].ap()]), "q_cc")
        p.po.wait(tcc)
        tsl2 = p.tok(nc.gpsimd.dma_start(out=sliv_in[:, :], in_=B["cc1_out"][0:128, :]), "q_mb", 16)
        p.ve.wait(tsl2)

        for cc in range(NCC):
            xs_ = exts[cc % 2]
            xs_.produce(p.po)
            ld = p.tok(nc.gpsimd.dma_start(out=ext[cc % 2][:, 8:8 + T], in_=B["xbc_sp"][cc]), "q_mb", 16)
            xs_.ready = ld
            p.ve.wait(ld)
            nc.vector.tensor_single_scalar(ext[cc % 2][:, 0:8], sliv_in[:, cc * 8:cc * 8 + 8],
                                           oddm[:, 0:1], ALU.mult)
            if cc > 0:
                p.ve.wait(t_si)
            nc.vector.tensor_single_scalar(cacc[:, :], ext[cc % 2][:, 5:5 + T],
                                           cw_sb[:, 0, cc:cc + 1], ALU.mult)
            for k in range(1, 4):
                nc.vector.scalar_tensor_tensor(cacc[:, :], ext[cc % 2][:, 5 + k:5 + k + T],
                                               cw_sb[:, k, cc:cc + 1], cacc[:, :],
                                               ALU.mult, ALU.add)
            t_cv = p.tok(nc.vector.tensor_copy(sp[:, 0:4], sp[:, 0:4]), "e_ve")
            xs_.frees.append(t_cv)
            p.se.wait(t_cv)
            if cc < 32:
                xo = xsts[cc % 2]
                xo.produce(p.se)
                t_si = p.tok(nc.scalar.activation(xst[cc % 2][:, :], cacc[:, :], AF.Silu,
                                                  bias=cb_sb[:, cc:cc + 1]), "e_se")
                xo.ready = t_si
                xo.consume(p.sy)
                if cc % 8 == 0:
                    xtmTs.produce(p.sy)
                for dq in range(max(1, T // 512)):
                    w_ = min(512, T)
                    nt_ = w_ // 128
                    tdt = p.tok(nc.sync.dma_start(
                        out=xtmT[:, dq * nt_:(dq + 1) * nt_, cc % 8, :],
                        in_=xst[cc % 2][:, dq * w_:(dq + 1) * w_], transpose=True), "q_dt", 16)
                xo.frees.append(tdt)
                if cc % 8 == 7:
                    g = cc // 8
                    p.po.wait(tdt)
                    for tc in range(NTC):
                        tst = p.tok(nc.gpsimd.dma_start(
                            out=B["xtm_sp"][tc * 128:(tc + 1) * 128, g * 1024:(g + 1) * 1024],
                            in_=xtmT[:, tc, :, :]), "q_st2", 16)
                    xtmTs.frees = [tst]
            else:
                t_si = p.tok(nc.scalar.activation(bc_fm[:, :], cacc[:, :], AF.Silu,
                                                  bias=cb_sb[:, cc:cc + 1]), "e_se")
        p.ve.wait(t_si)
        t_cf = p.tok(nc.vector.tensor_copy(c_fm[:, :], bc_fm[64:128, :]), "e_ve")
        B["t_cf"] = t_cf
        p.sy.wait(t_si)
        for dq in range(max(1, T // 512)):
            w_ = min(512, T)
            t_bt = p.tok(nc.sync.dma_start(out=b_tm[:, dq * 4:dq * 4 + w_ // 128, :],
                                           in_=bc_fm[0:64, dq * w_:(dq + 1) * w_],
                                           transpose=True), "q_dt", 16)

    phase_barrier()

    if MSTOP == "B2":
        esM.close()
        p.sb = p_sb_save
        return
    with contextlib.ExitStack() as esC:
        sbC = lambda n, s, d: esC.enter_context(nc.sbuf_tensor(n, s, d))
        xtm = [sbC(f"ms_x{i}", [128, I_ATT], BF16) for i in range(2)]
        xtms = [Slot() for _ in range(2)]
        xsq = sbC("ms_xsq", [128, 1024], BF16)
        scar = sbC("ms_S", [64, I_ATT], F32)

        nc.vector.memset(scar[:, :], 0.0)
        t_last_ad = p.tok(nc.vector.tensor_copy(sp[:, 0:4], sp[:, 0:4]), "e_ve")
        for c in range(NTC):
            xs_ = xtms[c % 2]
            xs_.produce(p.po)
            ld = p.tok(nc.gpsimd.dma_start(out=xtm[c % 2][:, :],
                                           in_=B["xtm_sp"][c * 128:(c + 1) * 128, :]), "q_mb", 16)
            xs_.ready = ld
            p.po.wait(t_last_ad)
            t_spl = p.tok(nc.gpsimd.dma_start(out=B["pst_sp"][c], in_=scar[:, :]), "q_st2", 16)
            p.ve.wait(t_spl)
            decb = bass.AP(dec_rep, c * 64, [[1024, 64], [1, 64], [0, 64]])
            t_up = p.tok(nc.vector.tensor_tensor(scar[:, :].rearrange("n (h pp) -> n h pp", h=64),
                                                 scar[:, :].rearrange("n (h pp) -> n h pp", h=64),
                                                 decb, ALU.mult), "e_ve")
            xs_.consume(p.ve)
            for q in range(NQ):
                dtdsb = bass.AP(dtds_bf, c * 64 + q * 16,
                                [[NTC * 64, 128], [1, 16], [0, 64]])
                t_xs = p.tok(nc.vector.tensor_tensor(
                    xsq[:, :].rearrange("p (h pp) -> p h pp", h=16),
                    xtm[c % 2][:, q * 1024:(q + 1) * 1024].rearrange("p (h pp) -> p h pp", h=16),
                    dtdsb, ALU.mult), "e_ve")
                if q == NQ - 1:
                    xs_.frees.append(t_xs)
                p.pe.wait(t_xs)
                p.pe.wait(t_bt)
                bank = 4 + q % 2
                p.pe.waits(B["psum_free"][bank]); p.pe.waits(B["psum_free"][bank + 2])
                B["psum_free"][bank] = []; B["psum_free"][bank + 2] = []
                nc.tensor.matmul(B["psum"][bank][0:64, :], b_tm[:, c, :], xsq[:, 0:512],
                                 start=True, stop=True)
                t_mm = p.tok(nc.tensor.matmul(B["psum"][bank + 2][0:64, :], b_tm[:, c, :],
                                              xsq[:, 512:1024], start=True, stop=True), "e_pe")
                p.ve.wait(t_mm)
                nc.vector.tensor_add(scar[:, q * 1024:q * 1024 + 512],
                                     scar[:, q * 1024:q * 1024 + 512], B["psum"][bank][0:64, :])
                t_ad = p.tok(nc.vector.tensor_add(scar[:, q * 1024 + 512:(q + 1) * 1024],
                                                  scar[:, q * 1024 + 512:(q + 1) * 1024],
                                                  B["psum"][bank + 2][0:64, :]), "e_ve")
                B["psum_free"][bank] = [t_ad]
                B["psum_free"][bank + 2] = [t_ad]
                t_last_ad = t_ad
        p.po.wait(t_last_ad)
        t_cs = p.tok(nc.gpsimd.dma_start(out=B["cc2_in"].ap(), in_=scar[:, :]), "q_mb", 16)
        p.po.wait(t_cs)
        t_cc2 = p.tok(nc.gpsimd.collective_compute(
            "AllGather", ALU.bypass, replica_groups=[[0, 1], [2, 3], [4, 5], [6, 7]],
            ins=[B["cc2_in"].ap()], outs=[B["cc2_out"].ap()]), "q_cc")
        p.po.wait(t_cc2)

    phase_barrier()

    if MSTOP == "B3":
        esM.close()
        p.sb = p_sb_save
        return
    with contextlib.ExitStack() as esD:
        sbD = lambda n, s, d: esD.enter_context(nc.sbuf_tensor(n, s, d))
        xtm = [sbD(f"my_x{i}", [128, I_ATT], BF16) for i in range(2)]
        xtms = [Slot() for _ in range(2)]
        gt = [sbD(f"my_g{i}", [128, I_ATT], BF16) for i in range(2)]
        gts = [Slot() for _ in range(2)]
        pvb = [sbD(f"my_p{i}", [64, I_ATT], BF16) for i in range(1)]
        pvs = [Slot() for _ in range(1)]
        pvc = sbD("my_pc", [64, I_ATT], BF16)
        carry = sbD("my_carry", [64, I_ATT], BF16)
        tcar = sbD("my_tcar", [64, I_ATT], BF16)
        repb = [sbD(f"my_rep{i}", [128, 2048], F32) for i in range(2)]
        reps = [Slot() for _ in range(2)]
        stm8 = sbD("my_stm", [128, 128], F32)
        dfq = sbD("my_dfq", [128, 2048], F32)
        df2 = sbD("my_df2", [128, 2048], F32)
        mtl = [sbD(f"my_mt{i}", [128, 2048], BF16) for i in range(2)]
        mtls = [Slot() for _ in range(2)]
        xwq = sbD("my_xwq", [128, 1024], BF16)
        y2b = sbD("my_y2", [128, I_ATT], BF16)
        tq1 = sbD("my_tq1", [128, 1024], F32)
        tq2 = sbD("my_tq2", [128, 1024], F32)
        gq = sbD("my_gq", [128, 1024], F32)
        sqq = sbD("my_sqq", [128, 1024], BF16)
        ys = sbD("my_ys", [128, 8], F32)
        ynst = [sbD(f"my_yn{i}", [128, 1024], BF16) for i in range(2)]
        ynss = [Slot() for _ in range(2)]
        g2rep = sbD("my_g2rep", [128, I_ATT], BF16)

        t_g2 = p.tok(nc.gpsimd.dma_start(
            out=g2rep[:, :], in_=bass.AP(B["d_nw"], 0, [[0, 128], [1, I_ATT]])), "q_mb", 16)
        t_cl = p.tok(nc.gpsimd.dma_start(out=carry[:, :], in_=B["cc2_out"][0:64, :]), "q_mb", 16)
        p.ve.wait(t_cl)
        t_cm = p.tok(nc.vector.tensor_single_scalar(carry[:, :], carry[:, :],
                                                    oddm[0:64, 0:1], ALU.mult), "e_ve")

        for c in range(NTC):
            xs_ = xtms[c % 2]
            xs_.produce(p.po)
            ldx = p.tok(nc.gpsimd.dma_start(out=xtm[c % 2][:, :],
                                            in_=B["xtm_sp"][c * 128:(c + 1) * 128, :]), "q_mb", 16)
            xs_.ready = ldx
            gs_ = gts[c % 2]
            gs_.produce(p.po)
            ldg = p.tok(nc.gpsimd.dma_start(out=gt[c % 2][:, :],
                                            in_=B["gate_sp"][c * 128:(c + 1) * 128, :]), "q_mb", 16)
            gs_.ready = ldg
            ps_ = pvs[0]
            ps_.produce(p.po)
            ldp = p.tok(nc.gpsimd.dma_start(out=pvb[0][:, :], in_=B["pst_sp"][c]), "q_mb", 16)
            ps_.ready = ldp
            ps_.consume(p.ve)
            chb = bass.AP(chain_rep, c * 64, [[1024, 64], [1, 64], [0, 64]])
            nc.vector.tensor_tensor(tcar[:, :].rearrange("n (h pp) -> n h pp", h=64),
                                    carry[:, :].rearrange("n (h pp) -> n h pp", h=64),
                                    chb, ALU.mult)
            t_pv = p.tok(nc.vector.tensor_add(pvc[:, :], pvb[0][:, :], tcar[:, :]), "e_ve")
            ps_.frees.append(t_pv)
            p.pe.waits(B["psum_free"][6]); B["psum_free"][6] = []
            p.pe.wait(B["t_cf"])
            t_st = p.tok(nc.tensor.matmul(B["psum"][6][:, 0:128],
                                          bc_fm[0:64, c * 128:(c + 1) * 128],
                                          c_fm[:, c * 128:(c + 1) * 128],
                                          start=True, stop=True), "e_pe")
            p.ve.wait(t_st)
            t_sm = p.tok(nc.vector.tensor_mul(stm8[:, :], B["psum"][6][:, 0:128], mask01[:, :]), "e_ve")
            B["psum_free"][6] = [t_sm]

            for r in range(NQ):
                rs_ = reps[r % 2]
                rs_.produce(p.sy)
                ldr = p.tok(nc.sync.dma_start(
                    out=repb[r % 2][:, :],
                    in_=bass.AP(B["acum_sp"], c * 8192 + r * 2048, [[0, 128], [1, 2048]])), "q_rep", 16)
                rs_.ready = ldr
                rs_.consume(p.ve)
                acb = bass.AP(acum_tm, c * 64 + r * 16, [[NTC * 64, 128], [1, 16], [0, 128]])
                nc.vector.tensor_tensor(dfq[:, :].rearrange("p (h l) -> p h l", h=16),
                                        repb[r % 2][:, :].rearrange("p (h l) -> p h l", h=16),
                                        acb, ALU.subtract)
                t_cl2 = p.tok(nc.vector.tensor_scalar_min(df2[:, :], dfq[:, :], 0.0), "e_ve")
                rs_.frees.append(t_cl2)
                p.se.wait(t_cl2)
                t_ex = p.tok(nc.scalar.activation(dfq[:, :], df2[:, :], AF.Exp), "e_se")
                mt_ = mtls[r % 2]
                mt_.produce(p.ve, [t_ex, t_sm])
                stb = bass.AP(stm8, 0, [[128, 128], [0, 16], [1, 128]])
                t_mt = p.tok(nc.vector.tensor_tensor(
                    mtl[r % 2][:, :].rearrange("p (h l) -> p h l", h=16),
                    dfq[:, :].rearrange("p (h l) -> p h l", h=16), stb, ALU.mult), "e_ve")
                mt_.ready = t_mt
                xs_.consume(p.ve)
                dtb2 = bass.AP(dt_bf, c * 64 + r * 16, [[NTC * 64, 128], [1, 16], [0, 64]])
                t_xw = p.tok(nc.vector.tensor_tensor(
                    xwq[:, :].rearrange("p (h pp) -> p h pp", h=16),
                    xtm[c % 2][:, r * 1024:(r + 1) * 1024].rearrange("p (h pp) -> p h pp", h=16),
                    dtb2, ALU.mult), "e_ve")
                p.pe.wait(t_pv)
                p.pe.wait(t_mt)
                p.pe.wait(t_xw)
                for bb in range(4):
                    p.pe.waits(B["psum_free"][bb]); B["psum_free"][bb] = []
                nc.tensor.matmul(B["psum"][2][:, :], c_fm[:, c * 128:(c + 1) * 128],
                                 pvc[:, r * 1024:r * 1024 + 512], start=True, stop=True)
                nc.tensor.matmul(B["psum"][3][:, :], c_fm[:, c * 128:(c + 1) * 128],
                                 pvc[:, r * 1024 + 512:(r + 1) * 1024], start=True, stop=True)
                for hh in range(16):
                    bank = hh // 8
                    ins = nc.tensor.matmul(
                        B["psum"][bank][:, (hh % 8) * 64:(hh % 8) * 64 + 64],
                        mtl[r % 2][:, hh * 128:(hh + 1) * 128],
                        xwq[:, hh * 64:(hh + 1) * 64], start=True, stop=True)
                t_ymm = p.tok(ins, "e_pe")
                mt_.frees.append(t_ymm)
                p.ve.wait(t_ymm)
                sdb = bass.AP(sd_tm, c * 64 + r * 16, [[NTC * 64, 128], [1, 16], [0, 64]])
                nc.vector.tensor_tensor(tq1[:, 0:512].rearrange("p (h pp) -> p h pp", h=8),
                                        B["psum"][2][:, :].rearrange("p (h pp) -> p h pp", h=8),
                                        bass.AP(sd_tm, c * 64 + r * 16, [[NTC * 64, 128], [1, 8], [0, 64]]),
                                        ALU.mult)
                nc.vector.tensor_tensor(tq1[:, 512:1024].rearrange("p (h pp) -> p h pp", h=8),
                                        B["psum"][3][:, :].rearrange("p (h pp) -> p h pp", h=8),
                                        bass.AP(sd_tm, c * 64 + r * 16 + 8, [[NTC * 64, 128], [1, 8], [0, 64]]),
                                        ALU.mult)
                nc.vector.tensor_add(tq2[:, 0:512], tq1[:, 0:512], B["psum"][0][:, :])
                t_t2 = p.tok(nc.vector.tensor_add(tq2[:, 512:1024], tq1[:, 512:1024],
                                                  B["psum"][1][:, :]), "e_ve")
                for bb in range(4):
                    B["psum_free"][bb] = [t_t2]
                db2 = bass.AP(d_repb, r * 16, [[64, 128], [1, 16], [0, 64]])
                nc.vector.tensor_tensor(tq1[:, :].rearrange("p (h pp) -> p h pp", h=16),
                                        xtm[c % 2][:, r * 1024:(r + 1) * 1024].rearrange(
                                            "p (h pp) -> p h pp", h=16), db2, ALU.mult)
                if r == NQ - 1:
                    xs_.frees.append(p.tok(nc.vector.tensor_copy(sp[:, 0:4], sp[:, 0:4]), "e_ve"))
                gs_.consume(p.se)
                t_gq = p.tok(nc.scalar.activation(gq[:, :], gt[c % 2][:, r * 1024:(r + 1) * 1024],
                                                  AF.Silu), "e_se")
                if r == NQ - 1:
                    gs_.frees.append(t_gq)
                nc.vector.tensor_add(tq2[:, :], tq2[:, :], tq1[:, :])
                p.ve.wait(t_gq)
                t_y2 = p.tok(nc.vector.tensor_mul(y2b[:, r * 1024:(r + 1) * 1024],
                                                  tq2[:, :], gq[:, :]), "e_ve")
                p.se.wait(t_y2)
                t_sq = p.tok(nc.scalar.activation(sqq[:, :], y2b[:, r * 1024:(r + 1) * 1024],
                                                  AF.Square), "e_se")
                p.ve.wait(t_sq)
                nc.vector.reduce_sum(ys[:, r:r + 1], sqq[:, :], axis=AX.X)
            spacer_ve()
            t_ys = p.tok(nc.vector.reduce_sum(ys[:, 4:5], ys[:, 0:4], axis=AX.X), "e_ve")
            spacer_ve()
            p.se.wait(t_ys)
            nc.scalar.activation(ys[:, 5:6], ys[:, 4:5], AF.Ln, bias=B["cst"][:, 0:1],
                                 scale=1.0 / I_ATT)
            nc.scalar.copy(st2[:, 7:8], ys[:, 4:5])
            t_rs = p.tok(nc.scalar.activation(ys[:, 6:7], ys[:, 5:6], AF.Exp, scale=-0.5), "e_se")
            p.ve.wait(t_rs)
            p.ve.wait(t_g2)
            for r in range(NQ):
                yn_ = ynss[r % 2]
                yn_.produce(p.ve)
                t_yn = p.tok(nc.vector.scalar_tensor_tensor(
                    ynst[r % 2][:, :], y2b[:, r * 1024:(r + 1) * 1024], ys[:, 6:7],
                    g2rep[:, r * 1024:(r + 1) * 1024], ALU.mult, ALU.mult), "e_ve")
                yn_.ready = t_yn
                yn_.consume(p.po)
                t_yst = p.tok(nc.gpsimd.dma_start(
                    out=B["ynm_sp"][c * 128:(c + 1) * 128, r * 1024:(r + 1) * 1024],
                    in_=ynst[r % 2][:, :]), "q_st2", 16)
                yn_.frees.append(t_yst)
                B["yn_toks"].append(t_yst)

    phase_barrier()

    if MSTOP == "C":
        esM.close()
        p.sb = p_sb_save
        return
    with contextlib.ExitStack() as esE:
        sbE = lambda n, s, d: esE.enter_context(nc.sbuf_tensor(n, s, d))
        ytm = sbE("mo_ytm", [128, I_ATT], BF16)
        ytms = Slot()
        yfm = sbE("mo_yfm", [128, 32, 512], BF16)
        yfms = Slot()
        wo = [sbE(f"mo_wo{i}", [128, 32, 256], BF16) for i in range(2)]
        wos = [Slot() for _ in range(2)]
        xo = [sbE(f"mo_xo{i}", [128, H], F32) for i in range(4)]
        xos = [Slot() for _ in range(4)]

        p.po.waits(B["yn_toks"])
        BW = min(4, NTC)
        for blk in range(NTC // BW):
            yfms.produce(p.sy)
            for ci in range(BW):
                c = blk * BW + ci
                ytms.produce(p.po)
                tld = p.tok(nc.gpsimd.dma_start(out=ytm[:, :],
                                                in_=B["ynm_sp"][c * 128:(c + 1) * 128, :]), "q_mb", 16)
                ytms.ready = tld
                ytms.consume(p.sy)
                for dq in range(8):
                    tdt = p.tok(nc.sync.dma_start(
                        out=yfm[:, 4 * dq:4 * dq + 4, ci * 128:(ci + 1) * 128],
                        in_=ytm[:, dq * 512:(dq + 1) * 512], transpose=True), "q_dt", 16)
                ytms.frees.append(tdt)
                yfms.ready = tdt
            p.pe.wait(yfms.ready)
            for f8 in range(8):
                ws_ = wos[f8 % 2]
                ws_.produce(p.po)
                tw = p.tok(nc.gpsimd.dma_start(
                    out=wo[f8 % 2][:, :, :],
                    in_=B["d_woT"].rearrange("(ic p) f -> p ic f", p=128)[:, :, f8 * 256:(f8 + 1) * 256]),
                    "q_wgu", 16)
                ws_.ready = tw
                ws_.consume(p.pe)
                for ci in range(BW):
                    bank = (ci % 2) * 2 + (f8 % 2)
                    p.pe.waits(B["psum_free"][bank]); B["psum_free"][bank] = []
                    for ic in range(32):
                        ins = nc.tensor.matmul(B["psum"][bank][:, 0:256],
                                               yfm[:, ic, ci * 128:(ci + 1) * 128],
                                               wo[f8 % 2][:, ic, :], start=(ic == 0), stop=(ic == 31))
                    t_mm = p.tok(ins, "e_pe")
                    if ci == BW - 1:
                        ws_.frees.append(t_mm)
                    if f8 == 7:
                        yfms.frees.append(t_mm)
                    c = blk * BW + ci
                    xs_ = xos[ci]
                    if f8 == 0:
                        xs_.produce(p.sy)
                        tld = p.tok(nc.sync.dma_start(out=xo[ci][:, :],
                                                      in_=xr_out[c * 128:(c + 1) * 128, :]), "q_ln", 16)
                        xs_.ready = tld
                    xs_.consume(p.ve)
                    p.ve.wait(t_mm)
                    t_ac = p.tok(nc.vector.tensor_add(
                        xo[ci][:, f8 * 256:(f8 + 1) * 256],
                        xo[ci][:, f8 * 256:(f8 + 1) * 256],
                        B["psum"][bank][:, 0:256]), "e_ve")
                    B["psum_free"][bank] = [t_ac]
                    if f8 == 7:
                        p.sy.wait(t_ac)
                        tst = p.tok(nc.sync.dma_start(
                            out=xr_out[c * 128:(c + 1) * 128, :], in_=xo[ci][:, :]), "q_st", 16)
                        xs_.frees.append(tst)
                        B["out_toks"].append(tst)

    phase_barrier()

    esM.close()
    p.sb = p_sb_save
    phase_barrier()


def _alloc_ffn_bufs(p, B, T, es2, tag):
    nc = p.nc
    sb = lambda n, s, d: es2.enter_context(nc.sbuf_tensor(tag + n, s, d))
    B["xr_tile"] = [sb(f"xrt{i}", [128, H], F32) for i in range(2)]
    B["xr_slot"] = [Slot() for _ in range(2)]
    B["lnt_tile"] = [sb(f"lnt{i}", [128, H], BF16) for i in range(2)]
    B["lnt_slot"] = [Slot() for _ in range(2)]
    B["xln_fm"] = sb("xln_fm", [128, 16, T // 2], BF16)
    B["xln_war"] = []
    B["lg"] = sb("lg", [128, 512], F32)
    B["lgb"] = sb("lgb", [128, 16, NE], F32)
    B["stm"] = sb("stm", [128, 16, 4], F32)
    B["dump"] = [sb("dumpa0", [128, H], BF16)]
    B["dump_slot"] = [Slot()]
    B["c_half"] = sb("c_half", [128, T // 256, NE], F32)
    B["acc"] = sb("acc", [128, T // 256, H], F32)
    B["wgu"] = [sb(f"wgu{i}", [128, 16, 512], BF16) for i in range(2)]
    B["wgu_slot"] = [Slot() for _ in range(2)]
    B["wdn"] = [sb(f"wdn{i}", [128, 8, 512], BF16) for i in range(2)]
    B["wdn_slot"] = [Slot() for _ in range(2)]
    B["sg"] = [sb("sg0", [128, 512], F32)]
    B["sg_slot"] = [Slot()]
    B["acttm"] = [sb(f"acttm{i}", [128, 512], BF16) for i in range(2)]
    B["acttm_slot"] = [Slot() for _ in range(2)]
    B["act_fm"] = sb("act_fm", [128, 8, T // 2], BF16)
    B["actfm_war"] = []
    B["wg_sb"] = sb("wg_sb", [128, 16, NE], BF16)
    tw = p.tok(nc.gpsimd.dma_start(
        out=B["wg_sb"][:, :, :],
        in_=B["d_wg"].rearrange("(dc p) e -> p dc e", p=128)), "q_wgu", 16)
    p.pe.wait(tw)
    nc.vector.memset(B["lg"][:, :], 0.0)
    nc.vector.memset(B["lgb"][:, :, :], 0.0)


def _phase_barrier(p):
    for eng in (p.pe, p.ve, p.se, p.po, p.sy):
        for s, v in list(p.counts.items()):
            eng.wait((s, v))


def build_program(T=2048, phases=("ffn1", "mamba", "ffn2")):
    nc = bass.Bass("TRN2", target_bir_lowering=False, debug=False)
    p = P(nc)

    x_in = nc.dram_tensor("x", [T, H], F32, kind="ExternalInput")
    out = nc.dram_tensor("out", [T, H], F32, kind="ExternalOutput")
    B = {}
    B["d_wg"] = nc.dram_tensor("w_gate", [H, NE], F32, kind="ExternalInput")
    w_gu_d = nc.dram_tensor("w_gu", [NE, H, 2 * I_MOE], F32, kind="ExternalInput")
    w_dn_d = nc.dram_tensor("w_dn", [NE, I_MOE, H], F32, kind="ExternalInput")
    B["w_sh_gu"] = nc.dram_tensor("w_sh_gu", [H, 2 * I_SH], F32, kind="ExternalInput")
    B["w_sh_dn"] = nc.dram_tensor("w_sh_dn", [I_SH, H], F32, kind="ExternalInput")
    B["w_gu"] = [w_gu_d[e] for e in range(NE)]
    B["w_dn"] = [w_dn_d[e] for e in range(NE)]
    B["d_winT"] = nc.dram_tensor("w_in_t", [H, PROJ], F32, kind="ExternalInput")
    B["d_woT"] = nc.dram_tensor("w_out_t", [I_ATT, H], F32, kind="ExternalInput")
    B["d_lnw"] = nc.dram_tensor("ln_w", [H], F32, kind="ExternalInput")
    B["d_nw"] = nc.dram_tensor("norm_w", [I_ATT], F32, kind="ExternalInput")
    B["d_alog"] = nc.dram_tensor("a_log", [64], F32, kind="ExternalInput")
    B["d_dtb"] = nc.dram_tensor("dt_b", [64], F32, kind="ExternalInput")
    B["d_dvec"] = nc.dram_tensor("d_vec", [64], F32, kind="ExternalInput")
    B["d_convw"] = nc.dram_tensor("conv_w4", [4, NCC, 128], F32, kind="ExternalInput")
    B["d_convb"] = nc.dram_tensor("conv_b2", [NCC, 128], F32, kind="ExternalInput")
    B["d_consts"] = nc.dram_tensor("consts", [128, 264], F32, kind="ExternalInput")

    xr1 = nc.dram_tensor("xr1", [T, H], F32)
    xr2 = nc.dram_tensor("xr2", [T, H], F32)
    B["gate_sp"] = nc.dram_tensor("gate_sp", [T, I_ATT], BF16)
    B["xbc_sp"] = nc.dram_tensor("xbc_sp", [NCC, 128, T], BF16)
    B["xtm_sp"] = nc.dram_tensor("xtm_sp", [T, I_ATT], BF16)
    B["pst_sp"] = nc.dram_tensor("pst_sp", [16, 64, I_ATT], BF16)
    B["acum_sp"] = nc.dram_tensor("acum_sp", [16, 8192], F32)
    B["bnd_sp"] = nc.dram_tensor("bnd_sp", [1024], F32)
    B["chain_sp"] = nc.dram_tensor("chain_sp", [1024], F32)
    B["ynm_sp"] = nc.dram_tensor("ynm_sp", [T, I_ATT], BF16)
    B["cc1_in"] = nc.dram_tensor("cc1_in", [128, NCC * 8], BF16)
    B["cc1_out"] = nc.dram_tensor("cc1_out", [256, NCC * 8], BF16)
    B["cc2_in"] = nc.dram_tensor("cc2_in", [64, I_ATT], F32)
    B["cc2_out"] = nc.dram_tensor("cc2_out", [128, I_ATT], F32)

    with p.es:
        B["psum"] = [p.ps(f"ps{i}", [128, 512], F32) for i in range(8)]
        B["psum_free"] = [[] for _ in range(8)]
        B["st2"] = p.sb("st2", [128, 8], F32)
        B["spacer"] = p.sb("spacer", [128, 4], F32)
        B["cst"] = p.sb("cst", [128, 8], F32)
        B["out_toks"] = []
        B["yn_toks"] = []
        nc.vector.memset(B["spacer"][:, :], 0.0)
        nc.vector.memset(B["cst"][:, 0:1], EPS)
        nc.vector.memset(B["cst"][:, 1:2], 1.0)

        chain = [x_in]
        if "ffn1" in phases:
            chain.append(xr1 if ("mamba" in phases or "ffn2" in phases) else out)
        if "mamba" in phases:
            chain.append(xr2 if "ffn2" in phases else out)
        if "ffn2" in phases:
            chain.append(out)

        ci = 0
        if "ffn1" in phases:
            with contextlib.ExitStack() as es2:
                _alloc_ffn_bufs(p, B, T, es2, "f1")
                build_ffn(p, B, chain[ci], chain[ci + 1], T, "f1")
                _phase_barrier(p)
            ci += 1
        if "mamba" in phases:
            build_mamba(p, B, chain[ci], chain[ci + 1], T)
            ci += 1
        if "ffn2" in phases:
            with contextlib.ExitStack() as es2:
                _alloc_ffn_bufs(p, B, T, es2, "f2")
                build_ffn(p, B, chain[ci], chain[ci + 1], T, "f2")
                _phase_barrier(p)
            ci += 1

        p.po.waits(B["out_toks"])
    return nc


def _mk_consts(core_id):
    c = np.zeros((128, 264), dtype=np.float32)
    c[:, 0:128] = np.eye(128, dtype=np.float32)
    l_ = np.arange(128)
    c[:, 128:256] = (l_[None, :] >= l_[:, None]).astype(np.float32)
    c[:, 256:264] = 1.0 if (core_id % 2 == 1) else 0.0
    return c


def kernel(L0_attn_hidden_states, L0_attn_ln_weight, L0_attn_in_proj_w,
           L0_attn_conv_w, L0_attn_conv_b, L0_attn_dt_bias, L0_attn_A_log,
           L0_attn_D, L0_attn_norm_weight, L0_attn_out_proj_w, L0_ffn_w_gate,
           L0_ffn_w_gate_up, L0_ffn_w_down, L0_ffn_w_sh_gate_up, L0_ffn_w_sh_down):
    T = 2048
    f32 = np.float32
    x = np.ascontiguousarray(np.asarray(L0_attn_hidden_states, dtype=f32))
    nc = build_program(T=T, phases=("ffn1", "mamba", "ffn2"))
    w_in_t = np.ascontiguousarray(np.asarray(L0_attn_in_proj_w, f32).T)
    w_out_t = np.ascontiguousarray(np.asarray(L0_attn_out_proj_w, f32).T)
    conv_w4 = np.ascontiguousarray(np.asarray(L0_attn_conv_w, f32).T.reshape(4, NCC, 128))
    conv_b2 = np.ascontiguousarray(np.asarray(L0_attn_conv_b, f32).reshape(NCC, 128))
    shared = dict(
        w_gate=np.ascontiguousarray(np.asarray(L0_ffn_w_gate, f32)),
        w_gu=np.ascontiguousarray(np.asarray(L0_ffn_w_gate_up, f32)),
        w_dn=np.ascontiguousarray(np.asarray(L0_ffn_w_down, f32)),
        w_sh_gu=np.ascontiguousarray(np.asarray(L0_ffn_w_sh_gate_up, f32)),
        w_sh_dn=np.ascontiguousarray(np.asarray(L0_ffn_w_sh_down, f32)),
        w_in_t=w_in_t, w_out_t=w_out_t,
        ln_w=np.ascontiguousarray(np.asarray(L0_attn_ln_weight, f32)),
        norm_w=np.ascontiguousarray(np.asarray(L0_attn_norm_weight, f32)),
        a_log=np.ascontiguousarray(np.asarray(L0_attn_A_log, f32)),
        dt_b=np.ascontiguousarray(np.asarray(L0_attn_dt_bias, f32)),
        d_vec=np.ascontiguousarray(np.asarray(L0_attn_D, f32)),
        conv_w4=conv_w4, conv_b2=conv_b2,
    )
    in_maps = []
    for c in range(8):
        b, th = c // 2, c % 2
        m = dict(shared)
        m["x"] = np.ascontiguousarray(x[b, th * T:(th + 1) * T, :])
        m["consts"] = _mk_consts(c)
        in_maps.append(m)
    res = run_bass_kernel_spmd(nc, in_maps, core_ids=list(range(8)))
    outs = np.stack([res.results[c]["out"] for c in range(8)])
    return outs.reshape(4, 4096, 2048)
